# version 03
# baseline (speedup 1.0000x reference)
import contextlib
import numpy as np
import concourse.bass as bass
import concourse.mybir as mybir
from concourse.bass_utils import run_bass_kernel_spmd
from concourse.aot_env import allow_ici_all_to_all

F32 = mybir.dt.float32
BF16 = mybir.dt.bfloat16
AF = mybir.ActivationFunctionType
ALU = mybir.AluOpType
AX = mybir.AxisListType

NCORES = 8
B, T, C = 2, 4096, 2048
NTOK = B * T
OWN = NTOK // NCORES
D = 64
I8 = 8192
GI = I8 // 128
NEG = -30000.0
SQ2I = 0.7071067811865476
RG = [list(range(NCORES))]
EPS = 1e-5
DEBUG = False


def _hmat(n):
    h = np.array([[1.0]])
    while h.shape[0] < n:
        h = np.block([[h, h], [h, -h]])
    return h.astype(np.float32)


def _own_rows(c):
    return np.concatenate([np.arange(s * 2048 + c * 256, s * 2048 + (c + 1) * 256)
                           for s in range(4)])


def _butterfly_sched(n):
    slot = list(range(n))
    spare = n
    ops = []
    h = 1
    while h < n:
        step = 2 * h
        for q in range(n // step):
            for r in range(h):
                ga, gb = q * step + r, q * step + h + r
                sa, sb = slot[ga], slot[gb]
                ops.append(('+', spare, sa, sb))
                ops.append(('-', sa, sa, sb))
                slot[ga], slot[gb], spare = spare, sa, sb
        h *= 2
    return ops, slot


BFS8, PERM8 = _butterfly_sched(GI)
BFS2, PERM2 = _butterfly_sched(16)


class Step:
    __slots__ = ("eng", "kind", "emit", "deps", "marked", "pos", "name")

    def __init__(self, eng, kind, emit, deps, name):
        self.eng, self.kind, self.emit, self.deps, self.name = \
            eng, kind, emit, deps, name
        self.marked = False
        self.pos = 0


class Res:
    __slots__ = ("w", "r", "name", "excl")

    def __init__(self, name, excl=False):
        self.w, self.r, self.name, self.excl = None, [], name, excl


class Sched:
    ENGS = ("tensor", "scalar", "vector", "gpsimd", "sync")

    def __init__(self):
        self.steps = []

    def step(self, eng, kind, emit, reads=(), writes=(), extra=(), name=""):
        deps = set()
        for r in reads:
            if r.w is not None:
                deps.add(r.w)
            if r.excl:
                deps.update(r.r)
        for r in writes:
            if r.w is not None:
                deps.add(r.w)
            deps.update(r.r)
        for d in extra:
            if d is not None:
                deps.add(d)
        stp = Step(eng, kind, emit, deps, name)
        self.steps.append(stp)
        for r in reads:
            r.r.append(stp)
        for r in writes:
            r.w = stp
            r.r = []
        return stp

    NDMASEM = 4

    def emit_all(self, nc, blk, stack):
        by_eng = {e: [s for s in self.steps if s.eng == e] for e in self.ENGS}
        self.semkey = {}
        ncoll = 0
        for e in self.ENGS:
            dmas = [s for s in by_eng[e] if s.kind == "dma"]
            for i, sd in enumerate(dmas):
                sd.marked = True
                self.semkey[id(sd)] = (e, "d", i % self.NDMASEM)
                if i >= self.NDMASEM:
                    sd.deps.add(dmas[i - self.NDMASEM])
        for stp in self.steps:
            if stp.kind == "coll":
                stp.marked = True
                self.semkey[id(stp)] = ("c", "c", ncoll)
                ncoll += 1
        for stp in self.steps:
            for d in stp.deps:
                if d.eng != stp.eng or d.kind != "compute":
                    d.marked = True
            if stp.kind == "compute":
                self.semkey[id(stp)] = (stp.eng, "x", 0)
        keys = sorted(set(self.semkey.values()))
        sems = {k: stack.enter_context(nc.semaphore(f"s_{k[0]}_{k[1]}{k[2]}"))
                for k in keys}
        cum = {k: 0 for k in keys}
        for stp in self.steps:
            if stp.marked:
                k = self.semkey[id(stp)]
                cum[k] += 16 if stp.kind == "dma" else 1
                stp.pos = cum[k]

        def emitter(eng_name):
            def go(eng):
                for stp in by_eng[eng_name]:
                    waits = {}
                    for d in stp.deps:
                        if d.eng == stp.eng and d.kind == "compute":
                            continue
                        k = self.semkey[id(d)]
                        waits[k] = max(waits.get(k, 0), d.pos)
                    for k, v in sorted(waits.items()):
                        eng.wait_ge(sems[k], v)
                    instr = stp.emit(eng)
                    if stp.marked:
                        assert instr is not None, f"step {stp.name} returned None"
                        instr.then_inc(sems[self.semkey[id(stp)]],
                                       16 if stp.kind == "dma" else 1)
            return go

        @blk.sync
        def _(sync):
            emitter("sync")(sync)

        @blk.gpsimd
        def _(gpsimd):
            emitter("gpsimd")(gpsimd)

        @blk.tensor
        def _(tensor):
            emitter("tensor")(tensor)

        @blk.scalar
        def _(scalar):
            emitter("scalar")(scalar)

        @blk.vector
        def _(vector):
            emitter("vector")(vector)


def build_nc():
    nc = bass.Bass("TRN2", target_bir_lowering=False, debug=False,
                   detect_race_conditions=False)
    stack = contextlib.ExitStack()
    S = Sched()
    st = S.step

    def din(name, shp, dty=F32):
        return nc.dram_tensor(name, shp, dty, kind="ExternalInput").ap()

    x_fm = din("x_fm", [C, OWN])
    x_tm = din("x_tm", [OWN, C])
    cos_q = din("cos_q", [128, OWN]); sin_q = din("sin_q", [128, OWN])
    cos_k = din("cos_k", [128, OWN]); sin_k = din("sin_k", [128, OWN])
    wqkv_d = din("wqkv", [24, 16, 128, 128])
    bqkv_d = din("bqkv", [128, 24])
    wo_d = din("wo_sl", [4, 64, 2048])
    bo_d = din("bo_b", [128, C])
    wup_d = din("wup_t", [GI, 16, 128, 128])
    wgate_d = din("wgate_t", [GI, 16, 128, 128])
    wdown_d = din("wdown_t", [4, GI, 128, 512])
    suug_d = din("su_ug_pp", [128, 16])
    sudp_d = din("su_dp_pp", [128, GI])
    wsu_d = din("ws_up_pp", [128, GI])
    wsg_d = din("ws_gate_pp", [128, GI])
    wsdp_d = din("ws_dp_b", [128, C])
    h128_d = din("h128c", [128, 128])
    prot_d = din("protTc", [128, 128])
    ident_d = din("identc", [128, 128])
    mask_d = din("masksc", [128, 4 * 512])
    eps_d = din("epsc", [128, 1])
    out_d = nc.dram_tensor("out_sh", [OWN, C], F32, kind="ExternalOutput").ap()

    if DEBUG:
        dbg_h1 = nc.dram_tensor("dbg_h1", [17 * 128, 512], F32)
        dbg_z = nc.dram_tensor("dbg_z", [65 * 128, 512], F32)
        dbg_pre = nc.dram_tensor("dbg_pre", [OWN, C], F32)
        rDbg = Res("dbg")
    a2a_in = nc.dram_tensor("a2a_in", [NCORES * 384, OWN], BF16)
    a2a_out = nc.dram_tensor("a2a_out", [NCORES * 384, OWN], BF16)
    rs_in = [nc.dram_tensor(f"rs_in{s}", [2048, C], F32) for s in range(4)]
    rs_out = [nc.dram_tensor(f"rs_out{s}", [256, C], F32) for s in range(4)]
    r1_dram = nc.dram_tensor("r1_dram", [OWN, C], F32)

    sbt = lambda name, shp, dty: stack.enter_context(nc.sbuf_tensor(name, shp, dty))
    pst = lambda name, shp: stack.enter_context(nc.psum_tensor(name, shp, F32))

    A1 = sbt("A1", [128, 16384], BF16)
    A2 = sbt("A2", [128, 16384], BF16)
    A3 = sbt("A3", [128, 8192], BF16)
    A4 = sbt("A4", [128, 8192], BF16)
    A5 = sbt("A5", [128, 8192], BF16)
    r1s = sbt("r1sb", [128, 8192], BF16)
    F1 = sbt("F1", [128, 4096], F32)
    F2 = sbt("F2", [128, 4096], F32)
    qln = sbt("qln", [128, C], F32)
    lnC = qln
    qstg = [type("V", (), {"ap": staticmethod(lambda q=qln, i=i: q.ap()[
        :, i * 1024:(i + 1) * 1024])})() for i in range(2)]
    rs_stage = [sbt(f"rs_stage{i}", [128, C], F32) for i in range(2)]
    masks = sbt("masksb", [128, 4 * 512], BF16)
    rres0 = sbt("rres0", [128, 512], F32)
    rres = [rres0, rres0]
    vscr = [sbt(f"vscr{i}", [128, 512], F32) for i in range(2)]
    dnst1 = sbt("dnst1", [128, 512], F32)
    hsu = sbt("hsu", [128, 16 * 128], BF16)
    ysc = [sbt(f"ysc{i}", [128, 512], BF16) for i in range(2)]
    zspare = sbt("zspare", [128, 512], BF16)
    h1spare = sbt("h1spare", [128, 512], BF16)
    h128_bf = sbt("h128_bf", [128, 128], BF16)
    h128_b6 = sbt("h128_b6", [128, 128], BF16)
    prot_f = sbt("prot_f", [128, 128], F32)
    ident_bf = sbt("ident_bf", [128, 128], BF16)
    bq_pp = sbt("bq_pp", [128, 24], F32)
    sudp_sb = sbt("sudp_sb", [128, GI], F32)
    wsu_sb = sbt("wsu_sb", [128, GI], F32)
    wsg_sb = sbt("wsg_sb", [128, GI], F32)
    suug_sb = sbt("suug_sb", [128, 16], F32)
    eps_t = sbt("eps_t", [128, 1], F32)
    ones_bf = sbt("ones_b", [128, 64], BF16)
    statb = sbt("statb", [128, 8], F32)

    psA = pst("psA", [128, 1024])
    psB = pst("psB", [128, 1024])
    ps4 = pst("ps4", [128, 512])
    ps5 = pst("ps5", [128, 512])
    ps6 = pst("ps6", [128, 512])
    ps7 = pst("ps7", [128, 512])

    rA1 = [Res(f"A1_{i}") for i in range(32)]
    rA2 = [Res(f"A2_{i}") for i in range(32)]
    rA3 = [Res(f"A3_{i}") for i in range(16)]
    rA4 = [Res(f"A4_{i}") for i in range(16)]
    rA5 = [Res(f"A5_{i}") for i in range(16)]
    rR1S = [Res(f"r1s_{i}") for i in range(16)]
    rF1 = [Res(f"F1_{i}") for i in range(2)]
    rF2 = [Res(f"F2_{i}") for i in range(2)]
    rLnC = Res("lnC")
    rQst = [Res(f"qst{i}") for i in range(2)]
    rRsst = [Res(f"rsst{i}") for i in range(2)]
    rMask = Res("mask"); rBo = Res("bo"); rWsdp = Res("wsdp")
    _rr = Res("rres0")
    rRres = [_rr, _rr]
    rVs = [Res(f"vscr{i}") for i in range(2)]
    rHsu = Res("hsu")
    rYsc = [Res(f"ysc{i}") for i in range(2)]
    rZsp = Res("zspare"); rH1sp = Res("h1spare")
    rH128 = Res("h128"); rH128b6 = Res("h128b6")
    rProt = Res("prot"); rIdent = Res("ident")
    rBq = Res("bq"); rSudp = Res("sudp"); rWsu = Res("wsu"); rWsg = Res("wsg")
    rSuug = Res("suug"); rEps = Res("eps"); rOnes = Res("ones"); rStat = Res("stat")
    rDnst1 = Res("dnst1")
    rPsA = [Res("psA0", True), Res("psA1", True)]
    rPsB = [Res("psB0", True), Res("psB1", True)]
    rPs4 = Res("ps4", True); rPs5 = Res("ps5", True)
    rPs6 = Res("ps6", True); rPs7 = Res("ps7", True)
    rA2A_in = Res("a2a_in"); rA2A_out = Res("a2a_out")
    rRsin = [Res(f"rsin{s}") for s in range(4)]
    rRsout = [Res(f"rsout{s}") for s in range(4)]
    rR1d = [Res(f"r1d{i}") for i in range(8)]
    rOut = Res("out")

    qh = [A1.ap()[:, 0:4096], A1.ap()[:, 4096:8192]]
    rQh = [rA1[0:8], rA1[8:16]]
    kdup = A1.ap()[:, 8192:12288]; rKd = rA1[16:24]
    vfm = A1.ap()[:, 12288:16384]; rVf = rA1[24:32]
    vtm = A2.ap()[:, 0:2080]; rVtm = rA2[0:5]
    probs = [A2.ap()[:, 2560:4608], A2.ap()[:, 4608:6656]]
    rProbs = [rA2[5:9], rA2[9:13]]
    a2st = [A2.ap()[:, 6656:7680], A2.ap()[:, 7680:8704]]
    rA2st = [rA2[13:15], rA2[15:17]]
    ao = [A3.ap()[:, i * 2048:(i + 1) * 2048] for i in range(4)]
    rAo = [rA3[2 * i:2 * i + 2] for i in range(4)]
    wo_bf = A4.ap()
    wdb = [A4.ap()[:, 0:4096], A4.ap()[:, 4096:8192]]
    rWdb = [rA4[0:8], rA4[8:16]]
    wjb = [A5.ap()[:, 0:2048], A5.ap()[:, 2048:4096]]
    rWjb = [rA5[0:4], rA5[4:8]]
    wfb = [A5.ap()[:, i * 2048:(i + 1) * 2048] for i in range(4)]
    rWfb = [rA5[4 * i:4 * i + 4] for i in range(4)]
    tb_cq = F1.ap()[:, 0:1024]; tb_ck = F1.ap()[:, 1024:2048]
    tb_sq = F2.ap()[:, 0:1024]; tb_sk = F2.ap()[:, 1024:2048]
    lnA = [F1.ap()[:, 0:2048], F1.ap()[:, 0:2048]]
    rLnA = [rF1[0], rF1[0]]
    lnB = [F2.ap()[:, 0:2048], F2.ap()[:, 0:2048]]
    rLnB = [rF2[0], rF2[0]]
    bo_sb = type("V", (), {"ap": staticmethod(lambda: F1.ap()[:, 2048:4096])})()
    wsdp_sb = type("V", (), {"ap": staticmethod(lambda: F2.ap()[:, 2048:4096])})()
    rBo = rF1[1]
    rWsdp = rF2[1]

    def zslot_ap(sl):
        if sl == GI:
            return zspare.ap()
        return (A1.ap()[:, sl * 512:(sl + 1) * 512] if sl < 32
                else A2.ap()[:, (sl - 32) * 512:(sl - 31) * 512])

    def zslot_res(sl):
        if sl == GI:
            return rZsp
        return rA1[sl] if sl < 32 else rA2[sl - 32]

    def h1slot_ap(sl):
        return h1spare.ap() if sl == 16 else A3.ap()[:, sl * 512:(sl + 1) * 512]

    def h1slot_res(sl):
        return rH1sp if sl == 16 else rA3[sl]

    def dma(eng, out_ap, in_ap, reads, writes, name):
        return st(eng, "dma",
                  lambda e, o=out_ap, i=in_ap: e.dma_start(out=o, in_=i),
                  reads=reads, writes=writes, name=name)

    for src, dstap, wres in [
            (cos_q, tb_cq, [rF1[0]]), (sin_q, tb_sq, [rF2[0]]),
            (cos_k, tb_ck, [rF1[0]]), (sin_k, tb_sk, [rF2[0]]),
            (eps_d, eps_t.ap(), [rEps]),
            (bo_d, bo_sb.ap(), [rBo]), (wsdp_d, wsdp_sb.ap(), [rWsdp]),
            (bqkv_d, bq_pp.ap(), [rBq]), (sudp_d, sudp_sb.ap(), [rSudp]),
            (wsu_d, wsu_sb.ap(), [rWsu]), (wsg_d, wsg_sb.ap(), [rWsg]),
            (suug_d, suug_sb.ap(), [rSuug])]:
        dma("sync", dstap[:, :], src[:, :], [], wres, "ldc")
    dma("gpsimd", prot_f.ap()[:, :], prot_d[:, :], [], [rProt], "ldp")
    dma("gpsimd", h128_bf.ap()[:, :], h128_d[:, :], [], [rH128], "ldh")
    dma("gpsimd", masks.ap()[:, :], mask_d[:, :], [], [rMask], "ldm")
    dma("gpsimd", ident_bf.ap()[:, :], ident_d[:, :], [], [rIdent], "ldi")
    for kb in range(16):
        dma("gpsimd", A1.ap()[:, kb * OWN:(kb + 1) * OWN],
            x_fm[kb * 128:(kb + 1) * 128, :], [],
            [rA1[2 * kb], rA1[2 * kb + 1]], f"xfm{kb}")

    st("vector", "compute", lambda e: e.memset(ones_bf.ap()[:, :], 1.0),
       writes=[rOnes], name="ones")
    st("vector", "compute",
       lambda e: e.memset(vtm.rearrange("p (t c) -> p t c", c=65)[:, :, 64:65], 1.0),
       writes=rVtm, name="vtm1")
    for g in range(16):
        st("vector", "compute",
           lambda e, g=g: e.tensor_scalar(
               hsu.ap()[:, g * 128:(g + 1) * 128], h128_bf.ap()[:, :],
               suug_sb.ap()[:, g:g + 1], 0.03125, ALU.mult, ALU.mult),
           reads=[rH128, rSuug], writes=[rHsu], name=f"hsu{g}")
    st("vector", "compute",
       lambda e: e.tensor_scalar(h128_b6.ap()[:, :], h128_bf.ap()[:, :],
                                 0.015625, None, ALU.mult),
       reads=[rH128], writes=[rH128b6], name="h6")
    for buf, res in [(wsu_sb, rWsu), (wsg_sb, rWsg), (wsdp_sb, rWsdp)]:
        st("vector", "compute",
           lambda e, b=buf: e.tensor_scalar(b.ap()[:, :], b.ap()[:, :],
                                            SQ2I, None, ALU.mult),
           reads=[res], writes=[res], name="wsfix")

    for hl in range(4):
        dma("gpsimd", wo_bf[0:64, hl * 2048:(hl + 1) * 2048], wo_d[hl],
            [], rA4[4 * hl:4 * hl + 4], f"ldwo{hl}")
    for jb in range(24):
        dma("gpsimd", wjb[jb % 2].rearrange("k (kb j) -> k kb j", kb=16),
            wqkv_d[jb].rearrange("kb k j -> k kb j"), [],
            rWjb[jb % 2], f"wq{jb}")
        for th in range(2):
            pp, rpp = [(ps4, rPs4), (ps5, rPs5)][th]

            def mm_qkv(e, jb=jb, th=th, pp=pp):
                last = None
                for kb in range(16):
                    last = e.matmul(
                        pp.ap()[:, :],
                        wjb[jb % 2][:, kb * 128:(kb + 1) * 128],
                        A1.ap()[:, kb * OWN + th * 512: kb * OWN + th * 512 + 512],
                        start=(kb == 0), stop=(kb == 15))
                return last
            st("tensor", "compute", mm_qkv,
               reads=list(rWjb[jb % 2]) + list(rA1), writes=[rpp],
               name=f"qkvmm{jb}_{th}")
            st("vector", "compute",
               lambda e, jb=jb, th=th, pp=pp: e.tensor_scalar_add(
                   qstg[jb % 2].ap()[:, th * 512: th * 512 + 512],
                   pp.ap()[:, :], bq_pp.ap()[:, jb:jb + 1]),
               reads=[rpp, rBq], writes=[rQst[jb % 2]], name=f"qkvev{jb}_{th}")
            if jb < 20:
                pr, rpr = [(ps6, rPs6), (ps7, rPs7)][th]
                st("tensor", "compute",
                   lambda e, jb=jb, th=th, pr=pr: e.matmul(
                       pr.ap()[:, :], prot_f.ap()[:, :],
                       qstg[jb % 2].ap()[:, th * 512: th * 512 + 512],
                       start=True, stop=True),
                   reads=[rProt, rQst[jb % 2]], writes=[rpr],
                   name=f"rot{jb}_{th}")
                cosb = tb_cq if jb < 16 else tb_ck
                sinb = tb_sq if jb < 16 else tb_sk

                def rope(e, jb=jb, th=th, cosb=cosb, sinb=sinb, pr=pr):
                    sl = slice(th * 512, (th + 1) * 512)
                    e.tensor_mul(vscr[0].ap()[:, :], pr.ap()[:, :], sinb[:, sl])
                    e.tensor_mul(vscr[1].ap()[:, :],
                                 qstg[jb % 2].ap()[:, sl], cosb[:, sl])
                    return e.tensor_add(a2st[jb % 2][:, sl],
                                        vscr[0].ap()[:, :], vscr[1].ap()[:, :])
                st("vector", "compute", rope,
                   reads=[rpr, rQst[jb % 2], rF1[0], rF2[0]],
                   writes=[rVs[0], rVs[1]] + list(rA2st[jb % 2]),
                   name=f"rope{jb}_{th}")
            else:
                st("vector", "compute",
                   lambda e, jb=jb, th=th: e.tensor_copy(
                       a2st[jb % 2][:, th * 512:(th + 1) * 512],
                       qstg[jb % 2].ap()[:, th * 512:(th + 1) * 512]),
                   reads=[rQst[jb % 2]], writes=list(rA2st[jb % 2]),
                   name=f"vcp{jb}_{th}")
        if jb < 16:
            j, off = jb // 2, (jb % 2) * 128
            dma("scalar", a2a_in.ap()[j * 384 + off: j * 384 + off + 128, :],
                a2st[jb % 2][:, :], list(rA2st[jb % 2]), [rA2A_in], f"a2w{jb}")
        elif jb < 20:
            i = jb - 16
            for hh in range(2):
                j = 2 * i + hh
                dma("scalar", a2a_in.ap()[j * 384 + 256: j * 384 + 320, :],
                    a2st[jb % 2][hh * 64:hh * 64 + 64, :],
                    list(rA2st[jb % 2]), [rA2A_in], f"a2wk{jb}_{hh}")
        else:
            i = jb - 20
            for hh in range(2):
                j = 2 * i + hh
                dma("scalar", a2a_in.ap()[j * 384 + 320: j * 384 + 384, :],
                    a2st[jb % 2][hh * 64:hh * 64 + 64, :],
                    list(rA2st[jb % 2]), [rA2A_in], f"a2wv{jb}_{hh}")

    def do_a2a(e):
        with allow_ici_all_to_all():
            return e.collective_compute(
                "AllToAll", ALU.bypass, replica_groups=RG,
                ins=[a2a_in.ap().opt()], outs=[a2a_out.ap().opt()])
    st("gpsimd", "coll", do_a2a, reads=[rA2A_in], writes=[rA2A_out], name="a2a")

    for bt in range(B):
        for ch in range(8):
            for s in (2 * bt, 2 * bt + 1):
                fo = (s % 2) * 2048 + ch * 256
                fos = fo // 512
                src0 = ch * 384
                ssl = slice(s * 256, (s + 1) * 256)
                dma("sync", qh[0][:, fo:fo + 256],
                    a2a_out.ap()[src0:src0 + 128, ssl], [rA2A_out],
                    [rQh[0][fos]], f"ldq0_{bt}_{ch}_{s}")
                dma("sync", qh[1][:, fo:fo + 256],
                    a2a_out.ap()[src0 + 128:src0 + 256, ssl], [rA2A_out],
                    [rQh[1][fos]], f"ldq1_{bt}_{ch}_{s}")
                dma("sync", kdup[0:64, fo:fo + 256],
                    a2a_out.ap()[src0 + 256:src0 + 320, ssl], [rA2A_out],
                    [rKd[fos]], f"ldka_{bt}_{ch}_{s}")
                dma("sync", kdup[64:128, fo:fo + 256],
                    a2a_out.ap()[src0 + 256:src0 + 320, ssl], [rA2A_out],
                    [rKd[fos]], f"ldkb_{bt}_{ch}_{s}")
                dma("sync", vfm[0:64, fo:fo + 256],
                    a2a_out.ap()[src0 + 320:src0 + 384, ssl], [rA2A_out],
                    [rVf[fos]], f"ldv_{bt}_{ch}_{s}")
        for vt in range(32):
            pw, rpw = [(ps6, rPs6), (ps7, rPs7)][vt % 2]
            st("tensor", "compute",
               lambda e, vt=vt, pw=pw: e.transpose(
                   pw.ap().bitcast(BF16)[:, 0:64],
                   vfm[0:64, vt * 128:(vt + 1) * 128],
                   ident_bf.ap()[0:64, 0:64]),
               reads=[rVf[vt // 4], rIdent], writes=[rpw], name=f"vT{bt}_{vt}")
            st("scalar", "compute",
               lambda e, vt=vt, pw=pw: e.copy(vtm[:, vt * 65: vt * 65 + 64],
                                              pw.ap().bitcast(BF16)[:, 0:64]),
               reads=[rpw], writes=[rVtm[min(vt // 7, 4)]], name=f"vTe{bt}_{vt}")
        for qc in range(8):
            qsl = slice(qc * 512, (qc + 1) * 512)
            aoc = (qc % 4) * 512
            for hp in range(2):
                for kc in range(qc + 1):
                    for rr in range(2):
                        def mm_sc(e, hp=hp, kc=kc, rr=rr, qsl=qsl):
                            last = None
                            for ks2 in range(2):
                                ks = 2 * rr + ks2
                                koff = kc * 512 + ks * 128
                                for hh in range(2):
                                    last = e.matmul(
                                        [psA, psB][hh].ap()[:, ks2 * 512:(ks2 + 1) * 512],
                                        kdup[hh * 64:hh * 64 + 64, koff:koff + 128],
                                        qh[hp][hh * 64:hh * 64 + 64, qsl],
                                        start=True, stop=True,
                                        tile_position=(hh * 64, 0),
                                        skip_group_check=True)
                            return last
                        st("tensor", "compute", mm_sc,
                           reads=[rKd[kc], rQh[hp][qc]],
                           writes=[rPsA[0], rPsA[1], rPsB[0], rPsB[1]],
                           name=f"sc{bt}_{hp}_{qc}_{kc}_{rr}")
                        def mm_exp(e, rr=rr):
                            last = None
                            for hh in range(2):
                                last = e.activation(
                                    probs[hh][:, rr * 1024:(rr + 1) * 1024],
                                    [psA, psB][hh].ap()[:, :], AF.Exp)
                            return last
                        st("scalar", "compute", mm_exp,
                           reads=[rPsA[0], rPsA[1], rPsB[0], rPsB[1]],
                           writes=[rProbs[0][2 * rr], rProbs[0][2 * rr + 1],
                                   rProbs[1][2 * rr], rProbs[1][2 * rr + 1]],
                           name=f"exp{bt}_{hp}_{qc}_{kc}_{rr}")
                        if kc == qc:
                            def mm_msk(e, rr=rr):
                                last = None
                                for ks2 in range(2):
                                    ks = 2 * rr + ks2
                                    for hh in range(2):
                                        sl = slice(rr * 1024 + ks2 * 512,
                                                   rr * 1024 + ks2 * 512 + 512)
                                        last = e.tensor_mul(
                                            probs[hh][:, sl], probs[hh][:, sl],
                                            masks.ap()[:, ks * 512:(ks + 1) * 512])
                                return last
                            st("vector", "compute", mm_msk,
                               reads=[rMask, rProbs[0][2 * rr],
                                      rProbs[0][2 * rr + 1],
                                      rProbs[1][2 * rr], rProbs[1][2 * rr + 1]],
                               writes=[rProbs[0][2 * rr], rProbs[0][2 * rr + 1],
                                       rProbs[1][2 * rr], rProbs[1][2 * rr + 1]],
                               name=f"msk{bt}_{hp}_{qc}_{rr}")

                        def mm_av(e, kc=kc, rr=rr, qc=qc):
                            last = None
                            for ks2 in range(2):
                                ks = 2 * rr + ks2
                                ktile = kc * 4 + ks
                                for hh in range(2):
                                    last = e.matmul(
                                        [ps4, ps5][hh].ap()[0:65, :],
                                        vtm[:, ktile * 65:(ktile + 1) * 65],
                                        probs[hh][:, rr * 1024 + ks2 * 512:
                                                  rr * 1024 + ks2 * 512 + 512],
                                        start=(kc == 0 and rr == 0 and ks2 == 0),
                                        stop=(kc == qc and rr == 1 and ks2 == 1),
                                        skip_group_check=True)
                            return last
                        st("tensor", "compute", mm_av,
                           reads=list(rVtm) + [rProbs[0][2 * rr],
                                               rProbs[0][2 * rr + 1],
                                               rProbs[1][2 * rr],
                                               rProbs[1][2 * rr + 1]],
                           writes=[rPs4, rPs5], name=f"av{bt}_{hp}_{qc}_{kc}_{rr}")
                for hh in range(2):
                    pv, rpv = [(ps4, rPs4), (ps5, rPs5)][hh]

                    st("vector", "compute",
                       lambda e, pv=pv: e.reciprocal(
                           vscr[0].ap()[64:65, :], pv.ap()[64:65, :]),
                       reads=[rpv], writes=[rVs[0]],
                       name=f"rr{bt}_{hp}_{qc}_{hh}")

                    def recsp(e):
                        e.tensor_copy(ysc[0].ap()[64:65, :], vscr[0].ap()[64:65, :])
                        e.tensor_copy(vscr[1].ap()[64:65, :], ysc[0].ap()[64:65, :])
                        e.tensor_sub(vscr[1].ap()[64:65, :],
                                     vscr[0].ap()[64:65, :],
                                     vscr[1].ap()[64:65, :])
                        return e.tensor_copy(ysc[1].ap()[64:65, :],
                                             vscr[1].ap()[64:65, :])
                    st("vector", "compute", recsp,
                       reads=[rVs[0]], writes=[rVs[1], rYsc[0], rYsc[1]],
                       name=f"rc{bt}_{hp}_{qc}_{hh}")

                    pw, rpw = [(ps6, rPs6), (ps7, rPs7)][hh]
                    bsc, rbsc = [(vscr[1], rVs[1]), (dnst1, rDnst1)][hh]

                    def bcast(e, pw=pw):
                        e.matmul(pw.ap()[0:64, :], ones_bf.ap()[64:65, 0:64],
                                 ysc[0].ap()[64:65, :], start=True, stop=False,
                                 tile_position=(64, 0), skip_group_check=True)
                        return e.matmul(pw.ap()[0:64, :],
                                        ones_bf.ap()[64:65, 0:64],
                                        ysc[1].ap()[64:65, :], start=False,
                                        stop=True, tile_position=(64, 0),
                                        skip_group_check=True)
                    st("tensor", "compute", bcast,
                       reads=[rOnes, rYsc[0], rYsc[1]], writes=[rpw],
                       name=f"bc{bt}_{hp}_{qc}_{hh}")
                    st("scalar", "compute",
                       lambda e, pw=pw, bsc=bsc: e.copy(
                           bsc.ap()[0:64, :], pw.ap()[0:64, :]),
                       reads=[rpw], writes=[rbsc], name=f"be{bt}_{hp}_{qc}_{hh}")
                    st("vector", "compute",
                       lambda e, pv=pv, hp=hp, hh=hh, aoc=aoc, bsc=bsc:
                       e.tensor_mul(
                           ao[hp * 2 + hh][0:64, aoc:aoc + 512],
                           pv.ap()[0:64, :], bsc.ap()[0:64, :]),
                       reads=[rpv, rbsc], writes=[rAo[hp * 2 + hh][aoc // 1024]],
                       name=f"dv{bt}_{hp}_{qc}_{hh}")
            if qc in (3, 7):
                s = 2 * bt + (0 if qc == 3 else 1)
                for tc in range(16):
                    tpos = tc * 128
                    pw, rpw = [(ps6, rPs6), (ps7, rPs7)][tc % 2]
                    for jh in range(4):
                        def mm_wo(e, tpos=tpos, pw=pw, jh=jh):
                            last = None
                            for hl in range(4):
                                last = e.matmul(
                                    pw.ap()[:, :],
                                    ao[hl][0:64, tpos: tpos + 128],
                                    wo_bf[0:64, hl * 2048 + jh * 512:
                                          hl * 2048 + jh * 512 + 512],
                                    start=(hl == 0), stop=(hl == 3),
                                    skip_group_check=True)
                            return last
                        st("tensor", "compute", mm_wo,
                           reads=[rAo[hl][tpos // 1024] for hl in range(4)]
                                 + list(rA4),
                           writes=[rpw], name=f"wo{s}_{tc}_{jh}")
                        st("scalar", "compute",
                           lambda e, tc=tc, jh=jh, pw=pw: e.copy(
                               rs_stage[tc % 2].ap()[:, jh * 512:(jh + 1) * 512],
                               pw.ap()[:, :]),
                           reads=[rpw], writes=[rRsst[tc % 2]],
                           name=f"we{s}_{tc}_{jh}")
                    dma("scalar", rs_in[s].ap()[tc * 128:(tc + 1) * 128, :],
                        rs_stage[tc % 2].ap()[:, :], [rRsst[tc % 2]],
                        [rRsin[s]], f"rw{s}_{tc}")
                st("gpsimd", "coll",
                   lambda e, s=s: e.collective_compute(
                       "ReduceScatter", ALU.add, replica_groups=RG,
                       ins=[rs_in[s].ap().opt()], outs=[rs_out[s].ap().opt()]),
                   reads=[rRsin[s]], writes=[rRsout[s]], name=f"rs{s}")

    for half in range(2):
        for lt in range(4):
            tt = half * 4 + lt
            s, sh = divmod(tt, 2)
            la, rla = lnA[lt % 2], rLnA[lt % 2]
            lb, rlb = lnB[lt % 2], rLnB[lt % 2]
            dma("sync", la[:, :], rs_out[s].ap()[sh * 128:(sh + 1) * 128, :],
                [rRsout[s]], [rla], f"lr{tt}")
            dma("sync", lb[:, :], x_tm[tt * 128:(tt + 1) * 128, :], [], [rlb],
                f"lx{tt}")

            def ln_pre(e, la=la, lb=lb):
                e.tensor_add(lnC.ap()[:, :], la[:, :], bo_sb.ap()[:, :])
                return e.tensor_add(la[:, :], lnC.ap()[:, :], lb[:, :])
            st("vector", "compute", ln_pre, reads=[rla, rlb, rBo],
               writes=[rLnC, rla], name=f"lp{tt}")
            st("vector", "compute",
               lambda e, la=la: e.reduce_sum(statb.ap()[:, 0:1], la[:, :],
                                             axis=AX.X),
               reads=[rla], writes=[rStat], name=f"ls{tt}")
            st("scalar", "compute",
               lambda e, la=la: e.activation(lnC.ap()[:, :], la[:, :],
                                             AF.Square,
                                             accum_out=statb.ap()[:, 1:2]),
               reads=[rla], writes=[rLnC, rStat], name=f"lq{tt}")

            def ln_stats(e):
                e.tensor_scalar(statb.ap()[:, 2:3], statb.ap()[:, 0:1],
                                1.0 / C, None, ALU.mult)
                e.tensor_scalar(statb.ap()[:, 3:4], statb.ap()[:, 1:2],
                                1.0 / C, None, ALU.mult)
                e.tensor_mul(statb.ap()[:, 4:5], statb.ap()[:, 2:3],
                             statb.ap()[:, 2:3])
                return e.tensor_sub(statb.ap()[:, 5:6], statb.ap()[:, 3:4],
                                    statb.ap()[:, 4:5])
            st("vector", "compute", ln_stats, reads=[rStat], writes=[rStat],
               name=f"lt{tt}")
            st("scalar", "compute",
               lambda e: e.activation(statb.ap()[:, 6:7], statb.ap()[:, 5:6],
                                      AF.Sqrt, bias=eps_t.ap()[:, 0:1]),
               reads=[rStat, rEps], writes=[rStat], name=f"lsd{tt}")
            st("vector", "compute",
               lambda e: e.reciprocal(statb.ap()[:, 7:8], statb.ap()[:, 6:7]),
               reads=[rStat], writes=[rStat], name=f"lrs{tt}")
            st("vector", "compute",
               lambda e, la=la, lb=lb: e.tensor_scalar(
                   lb[:, :], la[:, :], statb.ap()[:, 2:3], statb.ap()[:, 7:8],
                   ALU.subtract, ALU.mult),
               reads=[rla, rStat], writes=[rlb], name=f"la{tt}")
            dma("scalar", r1_dram.ap()[tt * 128:(tt + 1) * 128, :], lb[:, :],
                [rlb], [rR1d[tt]], f"r1w{tt}")
            st("vector", "compute",
               lambda e, lb=lb, lt=lt: e.tensor_copy(
                   r1s.ap()[:, lt * 2048:(lt + 1) * 2048], lb[:, :]),
               reads=[rlb], writes=rR1S[lt * 4:lt * 4 + 4], name=f"lb{tt}")

        for ct in range(64):
            lt, fb = divmod(ct, 16)
            pw, rpw = [(ps6, rPs6), (ps7, rPs7)][ct % 2]
            st("tensor", "compute",
               lambda e, lt=lt, fb=fb, pw=pw: e.transpose(
                   pw.ap().bitcast(BF16)[:, 0:128],
                   r1s.ap()[:, lt * 2048 + fb * 128: lt * 2048 + fb * 128 + 128],
                   ident_bf.ap()[:, :]),
               reads=[rR1S[lt * 4 + fb // 4], rIdent], writes=[rpw],
               name=f"ct{half}_{ct}")
            st("scalar", "compute",
               lambda e, lt=lt, fb=fb, pw=pw: e.copy(
                   A3.ap()[:, fb * 512 + lt * 128: fb * 512 + lt * 128 + 128],
                   pw.ap().bitcast(BF16)[:, 0:128]),
               reads=[rpw], writes=[rA3[fb]], name=f"ce{half}_{ct}")
        for g in range(16):
            pw, rpw = [(ps6, rPs6), (ps7, rPs7)][g % 2]
            st("tensor", "compute",
               lambda e, g=g, pw=pw: e.matmul(
                   pw.ap()[:, :], hsu.ap()[:, g * 128:(g + 1) * 128],
                   A3.ap()[:, g * 512:(g + 1) * 512], start=True, stop=True),
               reads=[rHsu, rA3[g]], writes=[rpw], name=f"f2m{half}_{g}")
            st("scalar", "compute",
               lambda e, g=g, pw=pw: e.copy(A3.ap()[:, g * 512:(g + 1) * 512],
                                            pw.ap()[:, :]),
               reads=[rpw], writes=[rA3[g]], name=f"f2e{half}_{g}")
        for op, dst, sa, sbb in BFS2:
            alu = ALU.add if op == '+' else ALU.subtract
            st("vector", "compute",
               lambda e, dst=dst, sa=sa, sbb=sbb, alu=alu: e.tensor_tensor(
                   h1slot_ap(dst)[:, :], h1slot_ap(sa)[:, :],
                   h1slot_ap(sbb)[:, :], alu),
               reads=[h1slot_res(sa), h1slot_res(sbb)],
               writes=[h1slot_res(dst)], name="bf2")
        if DEBUG and half == 0:
            for sl in range(17):
                st("vector", "compute",
                   lambda e, sl=sl: e.tensor_copy(vscr[0].ap()[:, :],
                                                  h1slot_ap(sl)[:, :]),
                   reads=[h1slot_res(sl)], writes=[rVs[0]], name=f"dh{sl}")
                dma("gpsimd", dbg_h1.ap()[sl * 128:(sl + 1) * 128, :],
                    vscr[0].ap()[:, :], [rVs[0]], [rDbg], f"dhw{sl}")
        for ib in range(GI):
            dma("gpsimd",
                wfb[2 * (ib % 2)].rearrange("k (kb j) -> k kb j", kb=16),
                wup_d[ib].rearrange("kb k j -> k kb j"), [],
                list(rWfb[2 * (ib % 2)]), f"wu{half}_{ib}")
            dma("gpsimd",
                wfb[2 * (ib % 2) + 1].rearrange("k (kb j) -> k kb j", kb=16),
                wgate_d[ib].rearrange("kb k j -> k kb j"), [],
                list(rWfb[2 * (ib % 2) + 1]), f"wg{half}_{ib}")
            ugb = [[(ps4.ap()[:, :], rPs4), (ps5.ap()[:, :], rPs5)],
                   [(psA.ap()[:, 0:512], rPsA[0]),
                    (psB.ap()[:, 0:512], rPsB[0])]][ib % 2]
            for which in range(2):
                pap, rpp = ugb[which]

                def mm_ug(e, ib=ib, which=which, pap=pap):
                    last = None
                    for kb in range(16):
                        last = e.matmul(
                            pap,
                            wfb[2 * (ib % 2) + which][:, kb * 128:(kb + 1) * 128],
                            h1slot_ap(PERM2[kb])[:, :],
                            start=(kb == 0), stop=(kb == 15),
                            skip_group_check=True)
                    return last
                st("tensor", "compute", mm_ug,
                   reads=list(rWfb[2 * (ib % 2) + which]) +
                         [h1slot_res(PERM2[kb]) for kb in range(16)],
                   writes=[rpp], name=f"ug{half}_{ib}_{which}")
            st("scalar", "compute",
               lambda e, ib=ib, pap=ugb[0][0]: e.activation(
                   ysc[0].ap()[:, :], pap, AF.Copy,
                   scale=wsu_sb.ap()[:, ib:ib + 1]),
               reads=[ugb[0][1], rWsu], writes=[rYsc[0]], name=f"ue{half}_{ib}")
            st("scalar", "compute",
               lambda e, ib=ib, pap=ugb[1][0]: e.activation(
                   ysc[1].ap()[:, :], pap, AF.Silu,
                   scale=wsg_sb.ap()[:, ib:ib + 1]),
               reads=[ugb[1][1], rWsg], writes=[rYsc[1]], name=f"ge{half}_{ib}")
            st("vector", "compute",
               lambda e, ib=ib: e.scalar_tensor_tensor(
                   zspare.ap()[:, :], ysc[0].ap()[:, :],
                   sudp_sb.ap()[:, ib:ib + 1], ysc[1].ap()[:, :],
                   ALU.mult, ALU.mult),
               reads=[rYsc[0], rYsc[1], rSudp], writes=[rZsp],
               name=f"y{half}_{ib}")
            pw, rpw = [(ps6, rPs6), (ps7, rPs7)][ib % 2]
            st("tensor", "compute",
               lambda e, pw=pw: e.matmul(pw.ap()[:, :], h128_b6.ap()[:, :],
                                         zspare.ap()[:, :], start=True,
                                         stop=True),
               reads=[rH128b6, rZsp], writes=[rpw], name=f"s1m{half}_{ib}")
            st("scalar", "compute",
               lambda e, ib=ib, pw=pw: e.copy(zslot_ap(ib)[:, :],
                                              pw.ap()[:, :]),
               reads=[rpw], writes=[zslot_res(ib)], name=f"s1e{half}_{ib}")
        for op, dst, sa, sbb in BFS8:
            alu = ALU.add if op == '+' else ALU.subtract
            st("vector", "compute",
               lambda e, dst=dst, sa=sa, sbb=sbb, alu=alu: e.tensor_tensor(
                   zslot_ap(dst)[:, :], zslot_ap(sa)[:, :],
                   zslot_ap(sbb)[:, :], alu),
               reads=[zslot_res(sa), zslot_res(sbb)],
               writes=[zslot_res(dst)], name="bf8")
        if DEBUG and half == 0:
            for sl in range(65):
                st("vector", "compute",
                   lambda e, sl=sl: e.tensor_copy(vscr[0].ap()[:, :],
                                                  zslot_ap(sl)[:, :]),
                   reads=[zslot_res(sl)], writes=[rVs[0]], name=f"dz{sl}")
                dma("gpsimd", dbg_z.ap()[sl * 128:(sl + 1) * 128, :],
                    vscr[0].ap()[:, :], [rVs[0]], [rDbg], f"dzw{sl}")
        pd_list = [(psA.ap()[:, 0:512], rPsA[0]), (psA.ap()[:, 512:1024], rPsA[1]),
                   (psB.ap()[:, 0:512], rPsB[0]), (psB.ap()[:, 512:1024], rPsB[1])]
        for jq in range(4):
            for gbb in range(8):
                dma("gpsimd",
                    wdb[gbb % 2].rearrange("p (g j) -> p g j", g=8),
                    wdown_d[jq, gbb * 8:(gbb + 1) * 8].rearrange(
                        "g p j -> p g j"), [], list(rWdb[gbb % 2]),
                    f"wd{half}_{jq}_{gbb}")
                for tsl in range(4):
                    pd_ap, pd_res = pd_list[tsl]

                    def mm_dn(e, gbb=gbb, tsl=tsl, pd_ap=pd_ap):
                        last = None
                        for gg in range(8):
                            g = gbb * 8 + gg
                            last = e.matmul(
                                pd_ap,
                                zslot_ap(PERM8[g])[:, tsl * 128:(tsl + 1) * 128],
                                wdb[gbb % 2][:, gg * 512:(gg + 1) * 512],
                                start=(g == 0), stop=(g == GI - 1),
                                skip_group_check=True)
                        return last
                    st("tensor", "compute", mm_dn,
                       reads=list(rWdb[gbb % 2]) +
                             [zslot_res(PERM8[gbb * 8 + gg]) for gg in range(8)],
                       writes=[pd_res], name=f"dn{half}_{jq}_{gbb}_{tsl}")
            for tsl in range(4):
                pd_ap, pd_res = pd_list[tsl]
                tt = half * 4 + tsl
                dma("sync", rres[tsl % 2].ap()[:, :],
                    r1_dram.ap()[tt * 128:(tt + 1) * 128,
                                 jq * 512:(jq + 1) * 512],
                    [rR1d[tt]], [rRres[tsl % 2]], f"rr{half}_{jq}_{tsl}")

                dpar = (jq * 4 + tsl) % 2
                dn_ap = vscr[1].ap() if dpar == 0 else dnst1.ap()
                dn_res = rVs[1] if dpar == 0 else rDnst1

                def dn_ev(e, tsl=tsl, jq=jq, pd_ap=pd_ap, dn_ap=dn_ap):
                    e.tensor_mul(vscr[0].ap()[:, :], pd_ap,
                                 wsdp_sb.ap()[:, jq * 512:(jq + 1) * 512])
                    return e.tensor_add(dn_ap[:, :], vscr[0].ap()[:, :],
                                        rres[tsl % 2].ap()[:, :])
                st("vector", "compute", dn_ev,
                   reads=[pd_res, rWsdp, rRres[tsl % 2]],
                   writes=[rVs[0], dn_res], name=f"de{half}_{jq}_{tsl}")
                dma("scalar",
                    r1_dram.ap()[tt * 128:(tt + 1) * 128,
                                 jq * 512:(jq + 1) * 512],
                    dn_ap[:, :], [dn_res], [rR1d[tt]],
                    f"dp{half}_{jq}_{tsl}")
        for tsl in range(4):
            tt = half * 4 + tsl
            rb, rrb = rs_stage[tsl % 2], rRsst[tsl % 2]
            ob, rob = lnC, rLnC
            dma("sync", rb.ap()[:, :],
                r1_dram.ap()[tt * 128:(tt + 1) * 128, :], [rR1d[tt]], [rrb],
                f"lpre{tt}")
            if DEBUG:
                dma("gpsimd", dbg_pre.ap()[tt * 128:(tt + 1) * 128, :],
                    rb.ap()[:, :], [rrb], [rDbg], f"dpw{tt}")
            st("vector", "compute",
               lambda e, rb=rb: e.reduce_sum(statb.ap()[:, 0:1], rb.ap()[:, :],
                                             axis=AX.X),
               reads=[rrb], writes=[rStat], name=f"2s{tt}")
            st("scalar", "compute",
               lambda e, rb=rb: e.activation(lnC.ap()[:, :], rb.ap()[:, :],
                                             AF.Square,
                                             accum_out=statb.ap()[:, 1:2]),
               reads=[rrb], writes=[rLnC, rStat], name=f"2q{tt}")

            def ln_stats2(e):
                e.tensor_scalar(statb.ap()[:, 2:3], statb.ap()[:, 0:1],
                                1.0 / C, None, ALU.mult)
                e.tensor_scalar(statb.ap()[:, 3:4], statb.ap()[:, 1:2],
                                1.0 / C, None, ALU.mult)
                e.tensor_mul(statb.ap()[:, 4:5], statb.ap()[:, 2:3],
                             statb.ap()[:, 2:3])
                return e.tensor_sub(statb.ap()[:, 5:6], statb.ap()[:, 3:4],
                                    statb.ap()[:, 4:5])
            st("vector", "compute", ln_stats2, reads=[rStat], writes=[rStat],
               name=f"2t{tt}")
            st("scalar", "compute",
               lambda e: e.activation(statb.ap()[:, 6:7], statb.ap()[:, 5:6],
                                      AF.Sqrt, bias=eps_t.ap()[:, 0:1]),
               reads=[rStat, rEps], writes=[rStat], name=f"2d{tt}")
            st("vector", "compute",
               lambda e: e.reciprocal(statb.ap()[:, 7:8], statb.ap()[:, 6:7]),
               reads=[rStat], writes=[rStat], name=f"2r{tt}")
            st("vector", "compute",
               lambda e, rb=rb, ob=ob: e.tensor_scalar(
                   ob.ap()[:, :], rb.ap()[:, :], statb.ap()[:, 2:3],
                   statb.ap()[:, 7:8], ALU.subtract, ALU.mult),
               reads=[rrb, rStat], writes=[rob], name=f"2a{tt}")
            dma("gpsimd", out_d[tt * 128:(tt + 1) * 128, :], ob.ap()[:, :],
                [rob], [rOut], f"st{tt}")

    st("gpsimd", "compute", lambda e: e.nop(), reads=[rOut], name="fin")

    blk = stack.enter_context(nc.Block())
    S.emit_all(nc, blk, stack)
    stack.close()
    return nc


def _stage_inputs(inputs):
    x = np.asarray(inputs["L0_attn_x"], dtype=np.float32).reshape(NTOK, C)
    Wq = np.asarray(inputs["L0_attn_Wq"], dtype=np.float32)
    bq = np.asarray(inputs["L0_attn_bq"], dtype=np.float32)
    Wk = np.asarray(inputs["L0_attn_Wk"], dtype=np.float32)
    bk = np.asarray(inputs["L0_attn_bk"], dtype=np.float32)
    Wv = np.asarray(inputs["L0_attn_Wv"], dtype=np.float32)
    bv = np.asarray(inputs["L0_attn_bv"], dtype=np.float32)
    Wo = np.asarray(inputs["L0_attn_Wo"], dtype=np.float32)
    bo = np.asarray(inputs["L0_attn_bo"], dtype=np.float32)
    SUu = np.asarray(inputs["L0_ffn_SU_ug"], dtype=np.float32)
    SUd = np.asarray(inputs["L0_ffn_SU_dp"], dtype=np.float32)
    Wsu = np.asarray(inputs["L0_ffn_Wscale_ug"], dtype=np.float32)
    Wsd = np.asarray(inputs["L0_ffn_Wscale_dp"], dtype=np.float32)
    Wup = np.asarray(inputs["L0_ffn_W_up"], dtype=np.float32)
    Wgate = np.asarray(inputs["L0_ffn_W_gate"], dtype=np.float32)
    Wdown = np.asarray(inputs["L0_ffn_W_down"], dtype=np.float32)

    WQKV = np.concatenate([Wq, Wk, Wv], axis=1)
    wqkv = np.ascontiguousarray(
        WQKV.reshape(16, 128, 24, 128).transpose(2, 0, 1, 3))
    bqkv = np.ascontiguousarray(np.concatenate([bq, bk, bv]).reshape(24, 128).T)
    bo_b = np.ascontiguousarray(np.broadcast_to(bo, (128, C)))
    wup_t = np.ascontiguousarray(
        Wup.T.reshape(16, 128, GI, 128).transpose(2, 0, 1, 3))
    wgate_t = np.ascontiguousarray(
        Wgate.T.reshape(16, 128, GI, 128).transpose(2, 0, 1, 3))
    wdown_t = np.ascontiguousarray(
        Wdown.T.reshape(GI, 128, 4, 512).transpose(2, 0, 1, 3))
    su_ug_pp = np.ascontiguousarray(SUu.reshape(16, 128).T)
    su_dp_pp = np.ascontiguousarray(SUd.reshape(GI, 128).T)
    ws_up_pp = np.ascontiguousarray(Wsu[:I8].reshape(GI, 128).T)
    ws_gate_pp = np.ascontiguousarray(Wsu[I8:].reshape(GI, 128).T)
    ws_dp_b = np.ascontiguousarray(np.broadcast_to(Wsd, (128, C)))

    h128 = _hmat(128)
    prot = np.zeros((64, 64), np.float32)
    for i2 in range(32):
        prot[2 * i2 + 1, 2 * i2] = -1.0
        prot[2 * i2, 2 * i2 + 1] = 1.0
    protT = np.zeros((128, 128), np.float32)
    protT[0:64, 0:64] = prot
    protT[64:128, 64:128] = prot
    ident = np.eye(128, dtype=np.float32)
    mask = np.zeros((128, 4 * 512), np.float32)
    for ks in range(4):
        kposl = ks * 128 + np.arange(128)[:, None]
        mask[:, ks * 512:(ks + 1) * 512] = np.where(
            kposl <= np.arange(512)[None, :], 1.0, 0.0)
    epsm = np.full((128, 1), EPS, np.float32)
    inv = 10000.0 ** (-np.arange(0, D, 2, dtype=np.float64) / D)

    per_core = []
    for c in range(NCORES):
        rows = _own_rows(c)
        pos = rows % T
        fr = np.outer(pos.astype(np.float64), inv)
        cosv = np.repeat(np.cos(fr), 2, axis=1).T
        sinv = np.repeat(np.sin(fr), 2, axis=1).T
        cos128 = np.ascontiguousarray(np.tile(cosv, (2, 1)).astype(np.float32))
        sin128 = np.ascontiguousarray(np.tile(sinv, (2, 1)).astype(np.float32))
        per_core.append({
            "x_fm": np.ascontiguousarray(x[rows].T),
            "x_tm": np.ascontiguousarray(x[rows]),
            "cos_q": cos128 * np.float32(0.125),
            "sin_q": sin128 * np.float32(0.125),
            "cos_k": cos128, "sin_k": sin128,
            "wqkv": wqkv, "bqkv": bqkv,
            "wo_sl": np.ascontiguousarray(
                Wo[c * 256:(c + 1) * 256, :].reshape(4, 64, 2048)),
            "bo_b": bo_b,
            "wup_t": wup_t, "wgate_t": wgate_t, "wdown_t": wdown_t,
            "su_ug_pp": su_ug_pp, "su_dp_pp": su_dp_pp,
            "ws_up_pp": ws_up_pp, "ws_gate_pp": ws_gate_pp, "ws_dp_b": ws_dp_b,
            "h128c": h128, "protTc": protT, "identc": ident,
            "masksc": mask, "epsc": epsm,
        })
    return per_core


def kernel(**inputs):
    per_core = _stage_inputs(inputs)
    nc = build_nc()
    res = run_bass_kernel_spmd(nc, per_core, core_ids=list(range(NCORES)))
    y = np.empty((NTOK, C), dtype=np.float32)
    for c in range(NCORES):
        y[_own_rows(c)] = res.results[c]["out_sh"]
    return y.reshape(B, T, C)
